# version 02
# baseline (speedup 1.0000x reference)
import numpy as np
import ml_dtypes
from contextlib import ExitStack

import concourse.bass as bass
import concourse.mybir as mybir
from concourse.bass_utils import run_bass_kernel_spmd

F32 = mybir.dt.float32
BF16 = mybir.dt.bfloat16
I32 = mybir.dt.int32
AF = mybir.ActivationFunctionType
ALU = mybir.AluOpType
AX = mybir.AxisListType

D = 4096
S = 4096
TPC = 1024
NEG = -1.0e30
EPS = 1e-5
ASCALE = 1.0 / np.sqrt(128.0)

ENGINES = ("sync", "scalar", "vector", "tensor", "gpsimd")



class H:
    __slots__ = ("eng", "kind", "val")

    def __init__(self, eng, kind, val):
        self.eng, self.kind, self.val = eng, kind, val


class Plan:

    def __init__(self, nc):
        self.nc = nc
        self.q = {e: [] for e in ENGINES}
        self.cnt = {}
        for e in ENGINES:
            self.cnt[(e, "c")] = 0
        self.settle_fn = None

    def op(self, eng, fn, deps=(), kind="c"):
        amt = 16 if isinstance(kind, tuple) else 1
        key = (eng, kind)
        self.cnt[key] = self.cnt.get(key, 0) + amt
        h = H(eng, kind, self.cnt[key])
        self.q[eng].append((fn, tuple(d for d in deps if d is not None), kind))
        return h

    def barrier(self, eng, deps):
        self.q[eng].append((lambda e: None,
                            tuple(d for d in deps if d is not None), "c"))

    def dma(self, eng, out, in_, deps=(), slot=None):
        if slot is None:
            to = out.tensor if hasattr(out, "tensor") else out
            if type(to).__name__ == "DRamTensorHandle":
                ti = in_.tensor if hasattr(in_, "tensor") else in_
                slot = "st_" + getattr(ti, "name", "anon")
            else:
                slot = getattr(to, "name", "anon")
        return self.op(eng, lambda e, o=out, i=in_: e.dma_start(out=o, in_=i),
                       deps, kind=("d", slot))

    def emit(self, ctx, block):
        sems = {}
        for (e, k), v in self.cnt.items():
            if v > 0:
                nm = f"sem_{e}_{k if isinstance(k, str) else 'd_' + k[1]}"
                sems[(e, k)] = ctx.enter_context(self.nc.semaphore(nm))

        def mk(e):
            def section(eng):
                waited = {}
                for fn, deps, kind in self.q[e]:
                    for dd in deps:
                        key = (dd.eng, dd.kind)
                        if waited.get(key, 0) >= dd.val:
                            continue
                        eng.wait_ge(sems[key], dd.val)
                        waited[key] = dd.val
                    inst = fn(eng)
                    if inst is not None:
                        if e == "tensor" and self.settle_fn is not None:
                            inst = self.settle_fn(eng)
                        inst.then_inc(sems[(e, kind)],
                                      16 if isinstance(kind, tuple) else 1)
            return section

        for e in ENGINES:
            if self.q[e]:
                getattr(block, e)(mk(e))
        return sems



def make_masks():
    i = np.arange(128)[:, None]
    j = np.arange(128)[None, :]
    mask_diag = np.where(j <= i, 0.0, NEG).astype(np.float32)
    mask_win = np.where(j > i, 0.0, NEG).astype(np.float32)
    return mask_diag, mask_win


def make_rope_tables(pos, scale):
    half = 64
    inv_freq = 10000.0 ** (-(np.arange(half, dtype=np.float64) / half))
    ang = pos[:, None].astype(np.float64) * inv_freq[None, :]
    cos = np.cos(ang).T
    sin = np.sin(ang).T
    cosf = np.concatenate([cos, cos], 0) * scale
    sinf = np.concatenate([sin, -sin], 0) * scale
    return cosf.astype(np.float32), sinf.astype(np.float32)



def alloc_stats(nc, ctx, tag):
    names = ["ssq", "ssum", "mean", "tmp", "vpe", "s0", "r0", "t2", "s1",
             "rstd", "nmr"]
    return {n: ctx.enter_context(nc.sbuf_tensor(f"{tag}_{n}", [128, 1], F32))
            for n in names}


def emit_ln(P, bufs, x_dram, xnt_sb, ident_sb, ident_h=None, extra_dep=()):
    ntiles = TPC // 128
    xs = bufs["xs"]
    st = bufs["stats"]
    xn_tm = bufs["xn_tm"]
    pt_ps = bufs["pt_ps"]
    ident = ident_sb
    last_xs = [None, None]
    last_tm = [None, None]
    last_pt = [None, None]
    prev_stats = None
    done = []
    for t in range(ntiles):
        bi = t % 2
        dep = list(extra_dep) + [last_xs[bi]]
        hld = P.dma("sync", xs[bi][:], x_dram[t * 128:(t + 1) * 128, :], dep)
        hsq = P.op("scalar", lambda e, b=bi: e.activation(
            xn_tm[b][:], xs[b][:], AF.Square, accum_out=st["ssq"][:]),
            [hld, prev_stats, last_tm[bi]])
        hsum = P.op("vector", lambda e, b=bi: e.reduce_sum(
            st["ssum"][:], xs[b][:], axis=AX.X), [hld, prev_stats])
        hmean = P.op("vector", lambda e: e.tensor_scalar(
            st["mean"][:], st["ssum"][:], 1.0 / D, None, ALU.mult), [hsum])
        htmp = P.op("vector", lambda e: e.tensor_scalar(
            st["tmp"][:], st["mean"][:], st["mean"][:], EPS,
            ALU.mult, ALU.subtract), [hmean])
        hvpe = P.op("vector", lambda e: e.tensor_scalar(
            st["vpe"][:], st["ssq"][:], 1.0 / D, st["tmp"][:],
            ALU.mult, ALU.subtract), [htmp, hsq])
        hs0 = P.op("scalar", lambda e: e.activation(
            st["s0"][:], st["vpe"][:], AF.Sqrt), [hvpe])
        hr0 = P.op("vector", lambda e: e.reciprocal(st["r0"][:], st["s0"][:]),
                   [hs0])
        ht2 = P.op("vector", lambda e: e.tensor_scalar(
            st["t2"][:], st["vpe"][:], st["r0"][:], None, ALU.mult),
            [hr0])
        hs1 = P.op("vector", lambda e: e.tensor_scalar(
            st["s1"][:], st["s0"][:], st["t2"][:], 0.5, ALU.add, ALU.mult),
            [ht2])
        hrstd = P.op("vector", lambda e: e.reciprocal(
            st["rstd"][:], st["s1"][:]), [hs1])
        hnmr = P.op("vector", lambda e: e.tensor_scalar(
            st["nmr"][:], st["mean"][:], st["rstd"][:], -1.0,
            ALU.mult, ALU.mult), [hrstd])
        hxn = P.op("scalar", lambda e, b=bi: e.activation(
            xn_tm[b][:], xs[b][:], AF.Identity, bias=st["nmr"][:],
            scale=st["rstd"][:]), [hnmr, last_tm[bi]])
        prev_stats = hxn
        last_xs[bi] = hxn
        for g in range(4):
            pi = (t * 4 + g) % 2

            def trg(e, b=bi, g=g, pi=pi):
                r = None
                for c in range(8):
                    kc = g * 8 + c
                    r = e.transpose(pt_ps[pi][:, c * 128:(c + 1) * 128],
                                    xn_tm[b][:, kc * 128:(kc + 1) * 128],
                                    ident[:])
                return r
            htr = P.op("tensor", trg, [hxn, last_pt[pi], ident_h])

            def evg(e, t=t, g=g, pi=pi):
                r = None
                for c in range(8):
                    kc = g * 8 + c
                    dst = xnt_sb[:, kc * TPC + t * 128: kc * TPC + (t + 1) * 128]
                    src = pt_ps[pi][:, c * 128:(c + 1) * 128]
                    r = e.tensor_copy(dst, src) if hasattr(e, "tensor_copy") \
                        else e.copy(dst, src)
                return r
            hev = P.op("vector" if g % 2 == 0 else "scalar", evg, [htr])
            last_pt[pi] = hev
            last_tm[bi] = hev
            done.append(hev)
    return done


def emit_proj_fm(P, bufs, wt_dram, out_dram, xnt_done, xnt_sb, n_out,
                 bias_sb=None, bias_h=None, scale=1.0, rope=None,
                 rope_dep=None, tag=""):
    nc = P.nc
    w_sb = bufs["w_fm"]
    ps = bufs["ps_fm"]
    ob = bufs["o_fm"]
    ra = bufs.get("rope_a")
    rb = bufs.get("rope_b")
    if "rope_war" not in bufs:
        bufs["rope_war"] = [None]
    last_w = bufs.setdefault("war_w_fm", [None, None])
    last_ps = bufs.setdefault("war_ps_fm", [None, None])
    last_ob = bufs.setdefault("war_o_fm", [None, None])
    single_w = len(w_sb) > 1 and (w_sb[0] is w_sb[1])
    nchunk = n_out // 128
    out_handles = []
    for oc in range(nchunk):
        wb = 0 if single_w else oc % 2
        hw = P.dma("gpsimd", w_sb[wb][:].rearrange("p (k c) -> p k c", k=32),
                   wt_dram.rearrange("(k p) o -> p k o", p=128)
                   [:, :, oc * 128:(oc + 1) * 128],
                   [last_w[wb]])
        tile_done = []
        for tt in range(TPC // 512):
            pi = (oc * (TPC // 512) + tt) % 2

            def mmg(e, wb=wb, tt=tt, pi=pi):
                r = None
                for kc in range(32):
                    r = e.matmul(ps[pi][:],
                                 w_sb[wb][:, kc * 128:(kc + 1) * 128],
                                 xnt_sb[:, kc * TPC + tt * 512:
                                        kc * TPC + (tt + 1) * 512],
                                 start=(kc == 0), stop=(kc == 31))
                return r
            hmm = P.op("tensor", mmg,
                       fences_list(xnt_done) + [hw, last_ps[pi]])
            if rope is None:
                bias_ap = (bias_sb[:, oc:oc + 1]
                           if bias_sb is not None else 0.0)
                deps = [hmm, last_ob[wb]]
                if bias_h is not None:
                    deps.append(bias_h)

                def ev(e, wb=wb, tt=tt, pi=pi, bias_ap=bias_ap):
                    return e.activation(
                        ob[wb][:, tt * 512:(tt + 1) * 512], ps[pi][:],
                        AF.Identity, bias=bias_ap, scale=scale)
                hev = P.op("scalar", ev, deps)
                last_ps[pi] = hev
                tile_done.append(hev)
            else:
                cos_sb, sin_sb = rope
                c_ap = cos_sb[:, tt * 512:(tt + 1) * 512]
                sL = sin_sb[0:64, tt * 512:(tt + 1) * 512]
                sH = sin_sb[64:128, tt * 512:(tt + 1) * 512]
                rw = bufs.get("rope_war")

                def r1(e, pi=pi, c_ap=c_ap):
                    return e.tensor_tensor(ra[:], ps[pi][:], c_ap, ALU.mult)
                h1 = P.op("vector", r1, [hmm, rw[0]] + list(rope_dep or []))

                def r2(e, pi=pi, sH=sH):
                    return e.tensor_tensor(rb[0:64, :], ps[pi][64:128, :], sH,
                                           ALU.mult)
                h2 = P.op("vector", r2, [hmm, rw[0]] + list(rope_dep or []))

                def r3(e, pi=pi, sL=sL):
                    return e.tensor_tensor(rb[64:128, :], ps[pi][0:64, :], sL,
                                           ALU.mult)
                h3 = P.op("vector", r3, [hmm, rw[0]] + list(rope_dep or []))

                def r4(e, wb=wb, tt=tt):
                    return e.tensor_tensor(ob[wb][:, tt * 512:(tt + 1) * 512],
                                           ra[:], rb[:], ALU.add)
                h4 = P.op("vector", r4, [h1, h2, h3, last_ob[wb]])
                rw[0] = h4
                last_ps[pi] = h4
                tile_done.append(h4)
        hst = P.dma("sync", out_dram[oc * 128:(oc + 1) * 128, :], ob[wb][:],
                    tile_done)
        last_ob[wb] = hst
        last_w[wb] = hmm
        out_handles.append(hst)
    return out_handles


def fences_list(handles):
    best = {}
    for h in handles:
        if h is None:
            continue
        k = (h.eng, h.kind)
        if k not in best or best[k].val < h.val:
            best[k] = h
    return list(best.values())


def emit_proj_tm(P, bufs, w_dram, out_dram, xnt_done, xnt_sb, n_out,
                 bias_row_sb=None, bias_h=None, ones_sb=None, ones_h=None,
                 tag=""):
    w_sb = bufs["w_tm"]
    single_w = w_sb[0] is w_sb[1] if len(w_sb) > 1 else True
    ps = bufs["ps_tm"]
    vb = bufs["v_tm"]
    last_w = bufs.setdefault("war_w_tm", [None, None])
    last_ps = bufs.setdefault("war_ps_tm", [None, None])
    last_vb = bufs.setdefault("war_v_tm", [None, None])
    out_handles = []
    nblk = n_out // 512
    for obk in range(nblk):
        wb = 0 if single_w else obk % 2
        hw = P.dma("gpsimd", w_sb[wb][:].rearrange("p (k c) -> p k c", k=32),
                   w_dram.rearrange("(k p) o -> p k o", p=128)
                   [:, :, obk * 512:(obk + 1) * 512],
                   [last_w[wb]])
        for tt in range(TPC // 128):
            pi = (obk * (TPC // 128) + tt) % 2

            def mmg(e, wb=wb, tt=tt, pi=pi, obk=obk):
                r = None
                if bias_row_sb is not None:
                    r = e.matmul(ps[pi][:], ones_sb[:],
                                 bias_row_sb[:, obk * 512:(obk + 1) * 512],
                                 start=True, stop=False)
                for kc in range(32):
                    r = e.matmul(ps[pi][:],
                                 xnt_sb[:, kc * TPC + tt * 128:
                                        kc * TPC + (tt + 1) * 128],
                                 w_sb[wb][:, kc * 512:(kc + 1) * 512],
                                 start=(kc == 0 and bias_row_sb is None),
                                 stop=(kc == 31))
                return r
            deps = fences_list(xnt_done) + [hw, last_ps[pi]]
            if bias_h is not None:
                deps += [bias_h, ones_h]
            hmm = P.op("tensor", mmg, deps)
            bj = tt % 2

            def ev(e, pi=pi, bj=bj):
                return e.tensor_copy(vb[bj][:], ps[pi][:]) \
                    if hasattr(e, "tensor_copy") else e.copy(vb[bj][:], ps[pi][:])
            hev = P.op("vector" if tt % 2 == 0 else "scalar", ev,
                       [hmm, last_vb[bj]])
            last_ps[pi] = hev
            hst = P.dma("sync",
                        out_dram[tt * 128:(tt + 1) * 128,
                                 obk * 512:(obk + 1) * 512],
                        vb[bj][:], [hev])
            last_vb[bj] = hst
            out_handles.append(hst)
        last_w[wb] = hmm
    return out_handles





def alloc_attn_bufs(nc, ctx, tag=""):
    sb = lambda n, s, d: ctx.enter_context(nc.sbuf_tensor(tag + n, s, d))
    psum = lambda n, s, d: ctx.enter_context(nc.psum_tensor(tag + n, s, d))
    names = ["negm", "negm2", "dtmp", "alpha", "zz", "zp", "rz", "negmsc"]
    fl = {n: sb("fl_" + n, [128, 1], F32) for n in names}
    return {
        "fl": fl,
        "oacc": sb("oacc", [128, 128], F32),
        "p_sb": [sb("p0", [128, 2048], BF16), sb("p1", [128, 2048], BF16)],
        "pt_sb": [sb("pt0", [128, 1024], BF16), sb("pt1", [128, 1024], BF16)],
        "ohead": [sb("oh0", [128, 32 * 128], BF16),
                  sb("oh1", [128, 32 * 128], BF16)],
        "s_ps": psum("s_ps", [128, 2048], F32),
        "pt_ps": [psum("ptp0", [128, 1024], BF16),
                  psum("ptp1", [128, 1024], BF16)],
        "o_ps": psum("o_ps", [128, 512], F32),
        "war": {},
    }


def _s_matmuls(e, ab, q_sb, k_sb, j, kstart, p0, chunks):
    r = None
    for (c0, cw) in chunks:
        r = e.matmul(ab["s_ps"][:, c0:c0 + cw],
                     q_sb[:, j * 128:(j + 1) * 128],
                     k_sb[:, kstart + p0 + c0: kstart + p0 + c0 + cw],
                     start=True, stop=True)
    return r


def _tr_group(e, ab, pp, gp, g0, nch, ident_sb):
    r = None
    for c in range(nch):
        r = e.transpose(ab["pt_ps"][gp][:, c * 128:(c + 1) * 128],
                        ab["p_sb"][pp][:, (g0 + c) * 128:(g0 + c + 1) * 128],
                        ident_sb[:])
    return r


def _av_group(e, ab, v_sb, gp, g0, nch, kbase_c, first, lastg):
    r = None
    for c in range(nch):
        kc = kbase_c + g0 + c
        r = e.matmul(ab["o_ps"][:, 0:128],
                     ab["pt_sb"][gp][:, c * 128:(c + 1) * 128],
                     v_sb[:, kc * 128:(kc + 1) * 128],
                     start=(first and c == 0),
                     stop=(lastg and c == nch - 1))
    return r


def emit_attn_head(P, ab, h, n_jtiles, q_sb, k_sb, v_sb, kv_dep, q_dep,
                   o_dram_ap, mask_diag_sb, mask_win_sb, mask_h, ident_sb,
                   ident_h, windowed, sscale=ASCALE, mask_h2=None):
    fl = ab["fl"]
    oacc = ab["oacc"]
    war = ab["war"]
    hp = h % 2
    ohead = ab["ohead"][hp]
    qk_reads = []
    v_reads = []
    for j in range(n_jtiles):
        Q0 = j * 128
        kend = Q0 + 128
        kstart = max(0, kend - 1152) if windowed else 0
        kn = kend - kstart
        pbounds = list(range(0, kn, 2048)) + [kn]
        npass = len(pbounds) - 1
        hZ = None
        hO = None
        halpha = None
        for ip in range(npass):
            first = ip == 0
            lastp = ip == npass - 1
            p0, p1 = pbounds[ip], pbounds[ip + 1]
            w = p1 - p0
            chunks = []
            c0 = 0
            while c0 < w:
                cw = min(512, w - c0)
                chunks.append((c0, cw))
                c0 += cw
            hs = P.op("tensor",
                      (lambda e, j=j, ks=kstart, p0=p0, ch=tuple(chunks):
                       _s_matmuls(e, ab, q_sb, k_sb, j, ks, p0, ch)),
                      [war.get("s_ps_free"), q_dep, ident_h] + list(kv_dep))
            qk_reads.append(hs)
            deps_red = [hs]
            if lastp:
                hmd = P.op("vector", lambda e, w=w: e.tensor_tensor(
                    ab["s_ps"][:, w - 128:w], ab["s_ps"][:, w - 128:w],
                    mask_diag_sb[:], ALU.add), [hs, mask_h])
                deps_red.append(hmd)
            if first and windowed and kstart > 0:
                hmw = P.op("vector", lambda e: e.tensor_tensor(
                    ab["s_ps"][:, 0:128], ab["s_ps"][:, 0:128],
                    mask_win_sb[:], ALU.add), [hs, mask_h2 or mask_h])
                deps_red.append(hmw)
            hred = P.op("vector", lambda e, w=w: e.reduce_max(
                fl["negm2"][:], ab["s_ps"][:, 0:w], axis=AX.X, negate=True),
                deps_red + [war.get("negm2_free")])
            if first:
                halpha = None
                hngm = P.op("vector", lambda e: e.tensor_copy(
                    fl["negm"][:], fl["negm2"][:]),
                    [hred, war.get("negm_free")])
            else:
                hdm = P.op("vector", lambda e: e.tensor_tensor(
                    fl["dtmp"][:], fl["negm"][:], fl["negm2"][:], ALU.min),
                    [hred, war.get("dtmp_free")])
                hdf = P.op("vector", lambda e: e.tensor_tensor(
                    fl["negm2"][:], fl["dtmp"][:], fl["negm"][:],
                    ALU.subtract), [hdm])
                halpha = P.op("scalar", lambda e: e.activation(
                    fl["alpha"][:], fl["negm2"][:], AF.Exp, scale=sscale),
                    [hdf, war.get("alpha_free")])
                hngm = P.op("vector", lambda e: e.tensor_copy(
                    fl["negm"][:], fl["dtmp"][:]),
                    [hdf, war.get("negm_free")])
                war["dtmp_free"] = hngm
            war["negm2_free"] = halpha or hngm
            pp = war.setdefault("pp_ctr", [0])[0] % 2
            war["pp_ctr"][0] += 1
            hnsc = P.op("vector", lambda e: e.tensor_scalar(
                fl["negmsc"][:], fl["negm"][:], sscale, None, ALU.mult),
                [hngm, war.get("negmsc_free")])
            hexp = P.op("scalar", lambda e, pp=pp, w=w: e.activation(
                ab["p_sb"][pp][:, 0:w], ab["s_ps"][:, 0:w], AF.Exp,
                scale=sscale, bias=fl["negmsc"][:], accum_out=fl["zp"][:]),
                [hnsc, halpha, war.get(("p_free", pp)), war.get("zp_free")])
            war["negmsc_free"] = hexp
            war["s_ps_free"] = hexp
            war["negm_free"] = hexp
            if first:
                hZ = P.op("vector", lambda e: e.tensor_copy(
                    fl["zz"][:], fl["zp"][:]), [hexp, war.get("zz_free")])
            else:
                hZ = P.op("vector", lambda e: e.scalar_tensor_tensor(
                    fl["zz"][:], fl["zz"][:], fl["alpha"][:], fl["zp"][:],
                    ALU.mult, ALU.add), [hexp, hZ, halpha])
            war["zp_free"] = hZ
            nkc = w // 128
            kbase_c = (kstart + p0) // 128
            hav = None
            htr_last = None
            first_av = True
            for g0 in range(0, nkc, 8):
                nch = min(8, nkc - g0)
                lastg = g0 + 8 >= nkc
                gp = war.setdefault("gp_ctr", [0])[0] % 2
                war["gp_ctr"][0] += 1
                htr = P.op("tensor",
                           (lambda e, pp=pp, gp=gp, g0=g0, nch=nch:
                            _tr_group(e, ab, pp, gp, g0, nch, ident_sb)),
                           [hexp, ident_h, war.get(("ptps_free", gp))])
                htr_last = htr
                hev = P.op("vector" if (g0 // 8) % 2 == 0 else "scalar",
                           (lambda e, gp=gp, nch=nch:
                            (e.tensor_copy if hasattr(e, "tensor_copy")
                             else e.copy)(ab["pt_sb"][gp][:, 0:nch * 128],
                                          ab["pt_ps"][gp][:, 0:nch * 128])),
                           [htr, war.get(("ptsb_free", gp))])
                war[("ptps_free", gp)] = hev
                hav = P.op("tensor",
                           (lambda e, gp=gp, g0=g0, nch=nch, kb=kbase_c,
                            fa=first_av, lg=lastg:
                            _av_group(e, ab, v_sb, gp, g0, nch, kb, fa, lg)),
                           [hev, war.get("o_ps_free")] + list(kv_dep))
                war[("ptsb_free", gp)] = hav
                v_reads.append(hav)
                first_av = False
            war[("p_free", pp)] = htr_last
            if first:
                hO = P.op("vector", lambda e: e.tensor_copy(
                    oacc[:], ab["o_ps"][:, 0:128]),
                    [hav, war.get("oacc_free")])
            else:
                hO = P.op("vector", lambda e: e.scalar_tensor_tensor(
                    oacc[:], oacc[:], fl["alpha"][:], ab["o_ps"][:, 0:128],
                    ALU.mult, ALU.add), [hav, hO, halpha])
                war["alpha_free"] = hO
            war["o_ps_free"] = hO
        hrz = P.op("vector", lambda e: e.reciprocal(
            fl["rz"][:], fl["zz"][:]), [hZ, war.get("rz_free")])
        war["zz_free"] = hrz
        hfin = P.op("vector", lambda e, j=j: e.tensor_scalar(
            ohead[:, j * 128:(j + 1) * 128], oacc[:], fl["rz"][:], None,
            ALU.mult), [hrz, hO, war.get(("ohead_free", hp))])
        war["rz_free"] = hfin
        war["oacc_free"] = hfin
        war.setdefault("fins", []).append(hfin)
    hdma = P.dma("sync", o_dram_ap,
                 ohead[:].rearrange("p (j c) -> p j c", j=n_jtiles),
                 war["fins"])
    war["fins"] = []
    war[("ohead_free", hp)] = hdma
    return qk_reads, v_reads, hdma



def build_launch1():
    nc = bass.Bass("TRN2", target_bir_lowering=False)
    x = nc.dram_tensor("x", [TPC, D], F32, kind="ExternalInput").ap()
    wqt = nc.dram_tensor("wqt", [D, D], F32, kind="ExternalInput").ap()
    wkt = nc.dram_tensor("wkt", [D, D], F32, kind="ExternalInput").ap()
    wvt = nc.dram_tensor("wvt", [D, D], F32, kind="ExternalInput").ap()
    bqt = nc.dram_tensor("bqt", [128, 32], F32, kind="ExternalInput").ap()
    bkt = nc.dram_tensor("bkt", [128, 32], F32, kind="ExternalInput").ap()
    bvr = nc.dram_tensor("bvr", [1, D], F32, kind="ExternalInput").ap()
    identity = nc.dram_tensor("identity", [128, 128], F32,
                              kind="ExternalInput").ap()
    qt = nc.dram_tensor("qt", [D, TPC], BF16, kind="ExternalOutput").ap()
    kt = nc.dram_tensor("kt", [D, TPC], BF16, kind="ExternalOutput").ap()
    v = nc.dram_tensor("v", [TPC, D], BF16, kind="ExternalOutput").ap()
    xnt = nc.dram_tensor("xnt", [D, TPC], BF16, kind="ExternalOutput").ap()

    with ExitStack() as ctx:
        sb = lambda n, s, d: ctx.enter_context(nc.sbuf_tensor(n, s, d))
        psum = lambda n, s, d: ctx.enter_context(nc.psum_tensor(n, s, d))
        bufs = {
            "xs": [sb("xs0", [128, D], F32), sb("xs1", [128, D], F32)],
            "stats": alloc_stats(nc, ctx, "st"),
            "xn_tm": [sb("xntm0", [128, D], BF16), sb("xntm1", [128, D], BF16)],
            "pt_ps": [psum("ptps0", [128, 1024], BF16),
                      psum("ptps1", [128, 1024], BF16)],
            "w_fm": [sb("wfm0", [128, 32 * 128], BF16),
                     sb("wfm1", [128, 32 * 128], BF16)],
            "ps_fm": [psum("psfm0", [128, 512], F32),
                      psum("psfm1", [128, 512], F32)],
            "o_fm": [sb("ofm0", [128, TPC], BF16), sb("ofm1", [128, TPC], BF16)],
            "w_tm": [sb("wtm0", [128, 32 * 512], BF16)] * 2,
            "ps_tm": [psum("pstm0", [128, 512], F32),
                      psum("pstm1", [128, 512], F32)],
            "v_tm": [sb("vtm0", [128, 512], BF16), sb("vtm1", [128, 512], BF16)],
        }
        xnt_sb = sb("xnt_sb", [128, 32 * TPC], BF16)
        ident_sb = sb("ident", [128, 128], BF16)
        bq_sb = sb("bq_sb", [128, 32], F32)
        bk_sb = sb("bk_sb", [128, 32], F32)
        bv_sb = sb("bv_sb", [1, D], F32)
        ones_sb = sb("ones_sb", [1, 128], F32)

        dscr = psum("dscr", [128, 1024], BF16)
        P = Plan(nc)
        P.settle_fn = lambda e: e.transpose(dscr[:, 0:128], ident_sb[:],
                                            ident_sb[:])
        hid = P.dma("gpsimd", ident_sb[:], identity)
        hbq = P.dma("sync", bq_sb[:], bqt)
        hbk = P.dma("sync", bk_sb[:], bkt)
        hbv = P.dma("sync", bv_sb[:], bvr)
        hones = P.op("vector", lambda e: e.memset(ones_sb[:], 1.0))

        ln_done = emit_ln(P, bufs, x, xnt_sb, ident_sb, ident_h=hid)
        lnf = fences_list(ln_done)
        hxnt = P.dma("sync", xnt.rearrange("(k p) t -> p k t", p=128),
                     xnt_sb[:].rearrange("p (k t) -> p k t", k=32), lnf)
        hq = emit_proj_fm(P, bufs, wqt, qt, lnf, xnt_sb, D,
                          bias_sb=bq_sb, bias_h=hbq, scale=1.0, tag="q")
        hk = emit_proj_fm(P, bufs, wkt, kt, lnf, xnt_sb, D,
                          bias_sb=bk_sb, bias_h=hbk, scale=1.0, tag="k")
        hv = emit_proj_tm(P, bufs, wvt, v, lnf, xnt_sb, D,
                          bias_row_sb=bv_sb, bias_h=hbv, ones_sb=ones_sb,
                          ones_h=hones, tag="v")
        P.barrier("sync", fences_list([hxnt] + hq + hk + hv))
        block = ctx.enter_context(nc.Block())
        P.emit(ctx, block)
    return nc



def _run(nc, in_maps):
    return run_bass_kernel_spmd(nc, in_maps, core_ids=list(range(8))).results


def kernel(L0_attn_x, L0_attn_Wq, L0_attn_bq, L0_attn_Wk, L0_attn_bk,
           L0_attn_Wv, L0_attn_bv, L0_attn_Wo, L0_attn_bo, L0_ffn_weight,
           L0_ffn_bias, L0_ffn_lora_A, L0_ffn_lora_B,
           L0_ffn_lora_start_indices, L1_attn_wq, L1_attn_wk, L1_attn_wv,
           L1_attn_wo, L1_ffn_A, L1_ffn_B, L1_ffn_Wr, L1_ffn_br):
    f32 = np.float32
    x = np.asarray(L0_attn_x, dtype=f32)
    Wq = np.asarray(L0_attn_Wq, dtype=f32); bq = np.asarray(L0_attn_bq, dtype=f32)
    Wk = np.asarray(L0_attn_Wk, dtype=f32); bk = np.asarray(L0_attn_bk, dtype=f32)
    Wv = np.asarray(L0_attn_Wv, dtype=f32); bv = np.asarray(L0_attn_bv, dtype=f32)
    Wo = np.asarray(L0_attn_Wo, dtype=f32); bo = np.asarray(L0_attn_bo, dtype=f32)
    Wf = np.asarray(L0_ffn_weight, dtype=f32)
    bfn = np.asarray(L0_ffn_bias, dtype=f32)
    lA = np.asarray(L0_ffn_lora_A, dtype=f32)
    lB = np.asarray(L0_ffn_lora_B, dtype=f32)
    lsi = np.asarray(L0_ffn_lora_start_indices).astype(np.int32).reshape(1, 4)
    wq1 = np.asarray(L1_attn_wq, dtype=f32)
    wk1 = np.asarray(L1_attn_wk, dtype=f32)
    wv1 = np.asarray(L1_attn_wv, dtype=f32)
    wo1 = np.asarray(L1_attn_wo, dtype=f32)
    A1 = np.asarray(L1_ffn_A, dtype=f32)
    B1 = np.asarray(L1_ffn_B, dtype=f32)
    Wr = np.asarray(L1_ffn_Wr, dtype=f32)
    br = np.asarray(L1_ffn_br, dtype=f32)

    mask_diag, mask_win = make_masks()
    ident = np.eye(128, dtype=f32)
    e8 = np.zeros((8, 128), dtype=f32)
    for h in range(8):
        e8[h, h * 16:(h + 1) * 16] = 1.0
    C = lambda a: np.ascontiguousarray(a)

    im = []
    for c in range(8):
        b, r = c // 4, c % 4
        im.append({"x": C(x[b, r * TPC:(r + 1) * TPC, :]),
                   "wqt": C(Wq.T), "wkt": C(Wk.T), "wvt": C(Wv.T),
                   "bqt": C(bq.reshape(32, 128).T),
                   "bkt": C(bk.reshape(32, 128).T),
                   "bvr": C(bv.reshape(1, -1)), "identity": ident})
    r1 = _run(build_launch1(), im)
    QT = [np.concatenate([r1[4 * b + r]["qt"] for r in range(4)], 1)
          for b in range(2)]
    KT = [np.concatenate([r1[4 * b + r]["kt"] for r in range(4)], 1)
          for b in range(2)]
    V = [np.concatenate([r1[4 * b + r]["v"] for r in range(4)], 0)
         for b in range(2)]

    im = []
    for c in range(8):
        b, r = c // 4, c % 4
        im.append({"q": C(QT[b][r * TPC:(r + 1) * TPC, :]),
                   "k": C(KT[b][r * TPC:(r + 1) * TPC, :]),
                   "v": C(V[b][:, r * TPC:(r + 1) * TPC]),
                   "mdg": mask_diag, "identity": ident})
    r2 = _run(build_launch2(), im)
    O = [np.concatenate([r2[4 * b + r]["o"] for r in range(4)], 1)
         for b in range(2)]

    im = []
    for c in range(8):
        b, r = c // 4, c % 4
        pos = np.arange(r * TPC, (r + 1) * TPC)
        rcos, rsin = make_rope_tables(pos, 1.0)
        im.append({
            "x": C(x[b, r * TPC:(r + 1) * TPC, :]),
            "xnt": r1[c]["xnt"],
            "ot": C(O[b][r * TPC:(r + 1) * TPC, :].T),
            "wot": C(Wo.T), "bor": C(bo.reshape(1, -1)),
            "wft": C(Wf.T), "bfr": C(bfn.reshape(1, -1)),
            "acat": C(lA.transpose(2, 0, 1).reshape(D, 48)),
            "bcat": C(lB.transpose(0, 2, 1).reshape(48, D)),
            "lsi": lsi, "bsel": np.array([[float(b)]], dtype=f32),
            "wq1": wq1, "wk1": wk1, "wv1": wv1,
            "rcos": rcos, "rsin": rsin, "identity": ident})
    r3 = _run(build_launch3(), im)
    Q1 = [np.concatenate([r3[4 * b + r]["q1t"] for r in range(4)], 1)
          for b in range(2)]
    K1 = [np.concatenate([r3[4 * b + r]["k1t"] for r in range(4)], 1)
          for b in range(2)]
    V1 = [np.concatenate([r3[4 * b + r]["v1"] for r in range(4)], 0)
          for b in range(2)]

    im = []
    for c in range(8):
        b, r = c // 4, c % 4
        im.append({"q": C(Q1[b][r * TPC:(r + 1) * TPC, :]),
                   "k": C(K1[b][r * 256:(r + 1) * 256, :]),
                   "v": C(V1[b][:, r * 256:(r + 1) * 256]),
                   "mdg": mask_diag, "mwn": mask_win, "identity": ident})
    r4 = _run(build_launch4(), im)
    O1 = [np.concatenate([r4[4 * b + r]["o"] for r in range(4)], 1)
          for b in range(2)]

    im = []
    for c in range(8):
        b, r = c // 4, c % 4
        im.append({
            "x1": r3[c]["x1"],
            "xn1t": r3[c]["xn1t"],
            "o1t": C(O1[b][r * TPC:(r + 1) * TPC, :].T),
            "wo1": wo1,
            "a1cat": C(A1.transpose(2, 0, 1).reshape(D, 128)),
            "b1cat": C(B1.transpose(0, 2, 1).reshape(128, D)),
            "wrt": C(Wr.T), "brr": C(br.reshape(1, 8)),
            "e8": e8, "identity": ident})
    r5 = _run(build_launch5(), im)

    out = np.empty((2, S, D), dtype=f32)
    for c in range(8):
        b, r = c // 4, c % 4
        out[b, r * TPC:(r + 1) * TPC, :] = r5[c]["out"]
    return out


def build_launch2():
    nc = bass.Bass("TRN2", target_bir_lowering=False)
    q = nc.dram_tensor("q", [8 * 128, S], BF16, kind="ExternalInput").ap()
    k = nc.dram_tensor("k", [8 * 128, S], BF16, kind="ExternalInput").ap()
    v = nc.dram_tensor("v", [S, 8 * 128], BF16, kind="ExternalInput").ap()
    mdg = nc.dram_tensor("mdg", [128, 128], F32, kind="ExternalInput").ap()
    identity = nc.dram_tensor("identity", [128, 128], F32,
                              kind="ExternalInput").ap()
    o = nc.dram_tensor("o", [S, 8 * 128], BF16, kind="ExternalOutput").ap()

    with ExitStack() as ctx:
        sb = lambda n, s, d: ctx.enter_context(nc.sbuf_tensor(n, s, d))
        ab = alloc_attn_bufs(nc, ctx)
        q_sb = [sb("q0", [128, S], BF16), sb("q1", [128, S], BF16)]
        k_sb = [sb("k0", [128, S], BF16), sb("k1", [128, S], BF16)]
        v_sb = [sb("v0", [128, 32 * 128], BF16), sb("v1", [128, 32 * 128], BF16)]
        md_sb = sb("md", [128, 128], F32)
        ident_sb = sb("ident", [128, 128], BF16)
        dscr = ctx.enter_context(nc.psum_tensor("dscr", [128, 1024], BF16))

        P = Plan(nc)
        P.settle_fn = lambda e: e.transpose(dscr[:, 0:128], ident_sb[:],
                                            ident_sb[:])
        hid = P.dma("gpsimd", ident_sb[:], identity)
        hmd = P.dma("gpsimd", md_sb[:], mdg)
        qk_free = [None, None]
        v_free = [None, None]
        odma = []
        for h in range(8):
            hb = h % 2
            hq = P.dma("sync", q_sb[hb][:], q[h * 128:(h + 1) * 128, :],
                       [qk_free[hb]])
            hk = P.dma("sync", k_sb[hb][:], k[h * 128:(h + 1) * 128, :],
                       [qk_free[hb]])
            hv = P.dma("sync", v_sb[hb][:].rearrange("p (kc c) -> p kc c", kc=32),
                       v.rearrange("(kc p) c -> p kc c", p=128)
                       [:, :, h * 128:(h + 1) * 128], [v_free[hb]])
            o_ap = o.rearrange("(j q) c -> q j c", q=128)[:, :, h * 128:(h + 1) * 128]
            qk_reads, v_reads, hdma = emit_attn_head(
                P, ab, h, 32, q_sb[hb], k_sb[hb], v_sb[hb],
                [hk, hv], hq, o_ap, md_sb, None, hmd,
                ident_sb, hid, windowed=False)
            qk_free[hb] = fences_list(qk_reads)[0] if qk_reads else None
            vf = fences_list(v_reads)
            v_free[hb] = vf[0] if vf else None
            odma.append(hdma)
        P.barrier("sync", fences_list(odma))
        block = ctx.enter_context(nc.Block())
        P.emit(ctx, block)
    return nc



def build_launch4():
    nc = bass.Bass("TRN2", target_bir_lowering=False)
    q = nc.dram_tensor("q", [8 * 128, S], BF16, kind="ExternalInput").ap()
    k = nc.dram_tensor("k", [2 * 128, S], BF16, kind="ExternalInput").ap()
    v = nc.dram_tensor("v", [S, 2 * 128], BF16, kind="ExternalInput").ap()
    mdg = nc.dram_tensor("mdg", [128, 128], F32, kind="ExternalInput").ap()
    mwn = nc.dram_tensor("mwn", [128, 128], F32, kind="ExternalInput").ap()
    identity = nc.dram_tensor("identity", [128, 128], F32,
                              kind="ExternalInput").ap()
    o = nc.dram_tensor("o", [S, 8 * 128], BF16, kind="ExternalOutput").ap()

    with ExitStack() as ctx:
        sb = lambda n, s, d: ctx.enter_context(nc.sbuf_tensor(n, s, d))
        ab = alloc_attn_bufs(nc, ctx)
        q_sb = [sb("q0", [128, S], BF16), sb("q1", [128, S], BF16)]
        k_sb = [sb("k0", [128, S], BF16), sb("k1", [128, S], BF16)]
        v_sb = [sb("v0", [128, 32 * 128], BF16), sb("v1", [128, 32 * 128], BF16)]
        md_sb = sb("md", [128, 128], F32)
        mw_sb = sb("mw", [128, 128], F32)
        ident_sb = sb("ident", [128, 128], BF16)
        dscr = ctx.enter_context(nc.psum_tensor("dscr", [128, 1024], BF16))

        P = Plan(nc)
        P.settle_fn = lambda e: e.transpose(dscr[:, 0:128], ident_sb[:],
                                            ident_sb[:])
        hid = P.dma("gpsimd", ident_sb[:], identity)
        hmd = P.dma("gpsimd", md_sb[:], mdg)
        hmw = P.dma("gpsimd", mw_sb[:], mwn)
        hks, hvs = [], []
        for kv in range(2):
            hks.append(P.dma("sync", k_sb[kv][:], k[kv * 128:(kv + 1) * 128, :]))
            hvs.append(P.dma(
                "sync", v_sb[kv][:].rearrange("p (kc c) -> p kc c", kc=32),
                v.rearrange("(kc p) c -> p kc c", p=128)
                [:, :, kv * 128:(kv + 1) * 128]))
        qbufs = [sb("qb0", [128, S], BF16), sb("qb1", [128, S], BF16)]
        q_free = [None, None]
        odma = []
        for h in range(8):
            hb = h % 2
            kv = h // 4
            hq = P.dma("sync", qbufs[hb][:], q[h * 128:(h + 1) * 128, :],
                       [q_free[hb]])
            o_ap = o.rearrange("(j q) c -> q j c", q=128)[:, :, h * 128:(h + 1) * 128]
            qk_reads, v_reads, hdma = emit_attn_head(
                P, ab, h, 32, qbufs[hb], k_sb[kv], v_sb[kv],
                [hks[kv], hvs[kv]], hq, o_ap, md_sb, mw_sb,
                hmd, ident_sb, hid, windowed=True, mask_h2=hmw)
            qf = fences_list(qk_reads)
            q_free[hb] = qf[0] if qf else None
            odma.append(hdma)
        P.barrier("sync", fences_list(odma))
        block = ctx.enter_context(nc.Block())
        P.emit(ctx, block)
    return nc


def build_launch3():
    nc = bass.Bass("TRN2", target_bir_lowering=False)
    di = lambda n, s, d: nc.dram_tensor(n, s, d, kind="ExternalInput").ap()
    do = lambda n, s, d: nc.dram_tensor(n, s, d, kind="ExternalOutput").ap()
    x = di("x", [TPC, D], F32)
    xnt = di("xnt", [D, TPC], BF16)
    ot = di("ot", [D, TPC], BF16)
    wot = di("wot", [D, D], F32)
    bor = di("bor", [1, D], F32)
    wft = di("wft", [D, D], F32)
    bfr = di("bfr", [1, D], F32)
    acat = di("acat", [D, 48], F32)
    bcat = di("bcat", [48, D], F32)
    lsi = di("lsi", [1, 4], I32)
    bsel = di("bsel", [1, 1], F32)
    wq1 = di("wq1", [D, D], F32)
    wk1 = di("wk1", [D, 1024], F32)
    wv1 = di("wv1", [D, 1024], F32)
    rcos = di("rcos", [128, TPC], F32)
    rsin = di("rsin", [128, TPC], F32)
    identity = di("identity", [128, 128], F32)
    x1 = do("x1", [TPC, D], F32)
    q1t = do("q1t", [D, TPC], BF16)
    k1t = do("k1t", [1024, TPC], BF16)
    v1 = do("v1", [TPC, 1024], BF16)
    xn1t = do("xn1t", [D, TPC], BF16)
    wdram = nc.dram_tensor("wdram", [1, 48], F32)

    with ExitStack() as ctx:
        sb = lambda n, s, d: ctx.enter_context(nc.sbuf_tensor(n, s, d))
        psum = lambda n, s, d: ctx.enter_context(nc.psum_tensor(n, s, d))
        xn1t_sb = sb("xn1t_sb", [128, 32 * TPC], BF16)
        xnt_tt = sb("xnt_tt", [128, 32 * 128], BF16)
        ot_tt = sb("ot_tt", [128, 32 * 128], BF16)
        wblk = sb("wblk", [128, 32 * 512], BF16)
        g_blk = sb("g_blk", [128, D], F32)
        xs0 = sb("xs0", [128, D], F32)
        xs1 = g_blk
        xntm0 = xnt_tt
        xntm1 = ot_tt
        rcos_sb = sb("rcos_sb", [128, TPC], F32)
        rsin_sb = sb("rsin_sb", [128, TPC], F32)
        wfm = sb("wfm", [128, 32 * 128], BF16)
        ofm0 = sb("ofm0", [128, TPC], BF16)
        ofm1 = sb("ofm1", [128, TPC], BF16)
        rhs0 = sb("rhs0", [128, 512], BF16)
        rhs1 = sb("rhs1", [128, 512], BF16)
        xt0 = sb("xt0", [128, 512], F32)
        xt1 = sb("xt1", [128, 512], F32)
        tmp0 = sb("tmp0", [128, 512], F32)
        tmp1 = sb("tmp1", [128, 512], F32)
        x1t0 = sb("x1t0", [128, 512], F32)
        x1t1 = sb("x1t1", [128, 512], F32)
        acat_sb = sb("acat_sb", [128, 32 * 48], BF16)
        bcat_sb = sb("bcat_sb", [48, D], BF16)
        rt_sb = sb("rt_sb", [48, TPC], BF16)
        w48_sb = sb("w48_sb", [48, 1], F32)
        bo_t = [sb("bo_t0", [1, 512], F32), sb("bo_t1", [1, 512], F32)]
        bf_t = [sb("bf_t0", [1, 512], F32), sb("bf_t1", [1, 512], F32)]
        ones_sb = sb("ones_sb", [1, 128], F32)
        ones16 = sb("ones16", [1, 16], F32)
        lsii = sb("lsii", [1, 4], I32)
        lsif = sb("lsif", [1, 4], F32)
        bsel_sb = sb("bsel_sb", [1, 1], F32)
        cnt_sb = sb("cnt_sb", [1, 1], F32)
        uc_sb = sb("uc_sb", [1, 1], F32)
        gate_sb = sb("gate_sb", [1, 1], F32)
        wa_sb = sb("wa_sb", [1, 3], F32)
        wfree = sb("wfree", [1, 48], F32)
        ident_sb = sb("ident_sb", [128, 128], BF16)
        stats = alloc_stats(nc, ctx, "st")
        ps_a = psum("ps_a", [128, 512], F32)
        ps_b = psum("ps_b", [128, 512], F32)
        ps_rt = psum("ps_rt", [48, 512], F32)
        ptps0 = psum("ptps0", [128, 1024], BF16)
        ptps1 = psum("ptps1", [128, 1024], BF16)
        ps_c = psum("ps_c", [128, 512], F32)
        ps_d = psum("ps_d", [128, 512], F32)
        dscr = psum("dscr", [128, 1024], BF16)

        P = Plan(nc)
        P.settle_fn = lambda e: e.transpose(dscr[:, 0:128], ident_sb[:],
                                            ident_sb[:])
        hid = P.dma("gpsimd", ident_sb[:], identity)

        hcos = P.dma("sync", rcos_sb[:], rcos)
        hsin = P.dma("sync", rsin_sb[:], rsin)
        hones = P.op("vector", lambda e: e.memset(ones_sb[:], 1.0))
        hones16 = P.op("vector", lambda e: e.memset(ones16[:], 1.0))
        hacat = P.dma("gpsimd", acat_sb[:].rearrange("p (k c) -> p k c", k=32),
                      acat.rearrange("(k p) c -> p k c", p=128))
        hbcat = P.dma("gpsimd", bcat_sb[:], bcat)

        hlsi = P.dma("sync", lsii[:], lsi)
        hbs = P.dma("sync", bsel_sb[:], bsel)
        hc0 = P.op("vector", lambda e: e.tensor_copy(lsif[:], lsii[:]), [hlsi])
        hc1 = P.op("vector", lambda e: e.tensor_scalar(
            lsif[:], lsif[:], bsel_sb[0:1, 0:1], None, ALU.is_le), [hc0, hbs])
        hc2 = P.op("vector", lambda e: e.reduce_sum(
            cnt_sb[:], lsif[:], axis=AX.X), [hc1])
        hc3 = P.op("vector", lambda e: e.tensor_scalar(
            uc_sb[:], cnt_sb[:], -2.0, 0.0, ALU.add, ALU.max), [hc2])
        hc4 = P.op("vector", lambda e: e.tensor_scalar(
            uc_sb[:], uc_sb[:], 2.0, None, ALU.min), [hc3])
        hc5 = P.op("vector", lambda e: e.tensor_scalar(
            gate_sb[:], cnt_sb[:], 2.0, None, ALU.is_ge), [hc2])
        hws = []
        for a in range(3):
            hw_ = P.op("vector", lambda e, a=a: e.tensor_scalar(
                wa_sb[0:1, a:a + 1], uc_sb[:], float(a),
                gate_sb[0:1, 0:1], ALU.is_equal, ALU.mult), [hc4, hc5])
            hws.append(hw_)
        hwf = []
        for a in range(3):
            hw2 = P.op("vector", lambda e, a=a: e.tensor_scalar(
                wfree[0:1, 16 * a:16 * (a + 1)], ones16[:],
                wa_sb[0:1, a:a + 1], None, ALU.mult), [hws[a], hones16])
            hwf.append(hw2)
        hwd = P.dma("sync", wdram.ap(), wfree[:], fences_list(hwf))
        hw48 = P.dma("sync", w48_sb[:], wdram.ap().rearrange("o (k c) -> k (o c)", k=48),
                     [hwd])

        rt_done = []
        hrt_prev = None
        for t2 in range(2):
            hmm = None
            for kc in range(32):
                rb_ = rhs0 if kc % 2 == 0 else rhs1
                hrl = P.dma("sync", rb_[:],
                            xnt[kc * 128:(kc + 1) * 128,
                                t2 * 512:(t2 + 1) * 512],
                            [P._war_rhs[kc % 2] if hasattr(P, "_war_rhs")
                             else None])
                if not hasattr(P, "_war_rhs"):
                    P._war_rhs = [None, None]
                hmm = P.op("tensor", lambda e, kc=kc, rb_=rb_: e.matmul(
                    ps_rt[:], acat_sb[:, kc * 48:(kc + 1) * 48], rb_[:],
                    start=(kc == 0), stop=(kc == 31)),
                    [hrl, hacat, hid, hrt_prev if kc == 0 else None])
                P._war_rhs[kc % 2] = hmm
            hev = P.op("vector", lambda e, t2=t2: e.tensor_scalar(
                rt_sb[:, t2 * 512:(t2 + 1) * 512], ps_rt[:], w48_sb[:],
                None, ALU.mult), [hmm, hw48])
            hrt_prev = hev
            rt_done.append(hev)
        hrt = fences_list(rt_done)

        war_xnt_tt = [None]
        war_ot_tt = [None]
        war_wblk = [None]
        war_ps = {"a": [None], "b": [None]}
        war_xt = [None, None]
        war_tmp = [None, None]
        war_x1t = [None, None]
        war_g = [None]
        x1_stores = []
        war_bf = [None, None]
        war_bo = [None, None]
        for ob in range(8):
            pb = ob % 2
            hwf_ = P.dma("gpsimd", wblk[:].rearrange("p (k c) -> p k c", k=32),
                         wft.rearrange("(k p) o -> p k o", p=128)
                         [:, :, ob * 512:(ob + 1) * 512], [war_wblk[0]])
            hbf = P.dma("sync", bf_t[pb][:],
                        bfr[0:1, ob * 512:(ob + 1) * 512], [war_bf[pb]])
            g_done = []
            for tt in range(8):
                hxl = P.dma("sync",
                            xnt_tt[:].rearrange("p (k c) -> p k c", k=32),
                            xnt.rearrange("(k p) t -> p k t", p=128)
                            [:, :, tt * 128:(tt + 1) * 128],
                            [war_xnt_tt[0]])
                pi = tt % 2
                pst = ps_a if pi == 0 else ps_b

                def ffnmm(e, tt=tt, ob=ob, pst=pst, pb=pb):
                    r = e.matmul(pst[:], ones_sb[:], bf_t[pb][:],
                                 start=True, stop=False)
                    for kc in range(32):
                        r = e.matmul(pst[:],
                                     xnt_tt[:, kc * 128:(kc + 1) * 128],
                                     wblk[:, kc * 512:(kc + 1) * 512],
                                     start=False, stop=False)
                    r = e.matmul(pst[:], rt_sb[:, tt * 128:(tt + 1) * 128],
                                 bcat_sb[:, ob * 512:(ob + 1) * 512],
                                 start=False, stop=True)
                    return r
                hmm = P.op("tensor", ffnmm,
                           [hxl, hwf_, war_ps["a" if pi == 0 else "b"][0],
                            hones, hbf, hbcat] + hrt)
                war_xnt_tt[0] = hmm
                hsil = P.op("scalar", lambda e, tt=tt, pst=pst: e.activation(
                    g_blk[:, tt * 512:(tt + 1) * 512], pst[:], AF.Silu),
                    [hmm, war_g[0]])
                war_ps["a" if pi == 0 else "b"][0] = hsil
                g_done.append(hsil)
            war_bf[pb] = hmm
            hwo_ = P.dma("gpsimd", wblk[:].rearrange("p (k c) -> p k c", k=32),
                         wot.rearrange("(k p) o -> p k o", p=128)
                         [:, :, ob * 512:(ob + 1) * 512],
                         [war_xnt_tt[0]])
            hbo = P.dma("sync", bo_t[pb][:],
                        bor[0:1, ob * 512:(ob + 1) * 512], [war_bo[pb]])
            g_all = fences_list(g_done)
            for tt in range(8):
                hol = P.dma("sync",
                            ot_tt[:].rearrange("p (k c) -> p k c", k=32),
                            ot.rearrange("(k p) t -> p k t", p=128)
                            [:, :, tt * 128:(tt + 1) * 128],
                            [war_ot_tt[0]])
                pi = tt % 2
                pst = ps_a if pi == 0 else ps_b

                def womm(e, tt=tt, ob=ob, pst=pst, pb=pb):
                    r = e.matmul(pst[:], ones_sb[:], bo_t[pb][:],
                                 start=True, stop=False)
                    for kc in range(32):
                        r = e.matmul(pst[:],
                                     ot_tt[:, kc * 128:(kc + 1) * 128],
                                     wblk[:, kc * 512:(kc + 1) * 512],
                                     start=False, stop=(kc == 31))
                    return r
                hmm = P.op("tensor", womm,
                           [hol, hwo_, war_ps["a" if pi == 0 else "b"][0],
                            hones, hbo])
                war_ot_tt[0] = hmm
                bi = tt % 2
                hxt = P.dma("sync", (xt0 if bi == 0 else xt1)[:],
                            x[tt * 128:(tt + 1) * 128,
                              ob * 512:(ob + 1) * 512], [war_xt[bi]])
                htm = P.op("vector", lambda e, tt=tt, pst=pst, bi=bi:
                           e.tensor_tensor((tmp0 if bi == 0 else tmp1)[:],
                                           pst[:],
                                           g_blk[:, tt * 512:(tt + 1) * 512],
                                           ALU.mult),
                           [hmm, war_tmp[bi]] + g_all)
                war_ps["a" if pi == 0 else "b"][0] = htm
                hx1 = P.op("vector", lambda e, bi=bi: e.tensor_tensor(
                    (x1t0 if bi == 0 else x1t1)[:],
                    (tmp0 if bi == 0 else tmp1)[:],
                    (xt0 if bi == 0 else xt1)[:], ALU.add),
                    [htm, hxt, war_x1t[bi]])
                war_tmp[bi] = hx1
                war_xt[bi] = hx1
                hst = P.dma("sync", x1[tt * 128:(tt + 1) * 128,
                                       ob * 512:(ob + 1) * 512],
                            (x1t0 if bi == 0 else x1t1)[:], [hx1])
                war_x1t[bi] = hst
                x1_stores.append(hst)
            war_bo[pb] = hmm
            war_wblk[0] = hmm
            war_g[0] = fences_list([war_ps["a"][0], war_ps["b"][0]])[0]
        hx1f = fences_list(x1_stores)

        ln_bufs = {
            "xs": [xs0, xs1], "stats": stats,
            "xn_tm": [xntm0, xntm1],
            "pt_ps": [ptps0, ptps1],
        }
        ln_done = emit_ln(P, ln_bufs, x1, xn1t_sb, ident_sb, ident_h=hid,
                          extra_dep=hx1f)
        lnf = fences_list(ln_done)
        hxn1 = P.dma("sync", xn1t.rearrange("(k p) t -> p k t", p=128),
                     xn1t_sb[:].rearrange("p (k t) -> p k t", k=32), lnf)

        pbufs = {
            "w_fm": [wfm, wfm],
            "ps_fm": [ps_c, ps_d],
            "o_fm": [ofm0, ofm1],
            "rope_a": tmp0, "rope_b": tmp1,
            "w_tm": [wblk, wblk],
            "ps_tm": [ps_a, ps_b],
            "v_tm": [rhs0, rhs1],
            "war_w_fm": [None, None], "war_ps_fm": [None, None],
            "war_o_fm": [None, None],
            "war_w_tm": [fences_list([war_wblk[0]])[0], None],
            "war_ps_tm": [war_ps["a"][0], war_ps["b"][0]],
            "war_v_tm": [P._war_rhs[0], P._war_rhs[1]],
            "rope_war": [fences_list([war_tmp[0], war_tmp[1]])[0]],
        }
        hv1 = emit_proj_tm(P, pbufs, wv1, v1, lnf, xn1t_sb, 1024, tag="v1")
        pbufs["war_w_fm"] = [None, None]
        hrope = [hcos, hsin]
        hq1 = emit_proj_fm(P, pbufs, wq1, q1t, lnf, xn1t_sb, D,
                           rope=(rcos_sb, rsin_sb), rope_dep=hrope, tag="q1")
        hk1 = emit_proj_fm(P, pbufs, wk1, k1t, lnf, xn1t_sb, 1024,
                           rope=(rcos_sb, rsin_sb), rope_dep=hrope, tag="k1")
        P.barrier("sync", fences_list([hxn1] + hv1 + hq1 + hk1 +
                                      x1_stores + [hcos, hsin, hid]))
        block = ctx.enter_context(nc.Block())
        P.emit(ctx, block)
    return nc


def build_launch5():
    nc = bass.Bass("TRN2", target_bir_lowering=False)
    di = lambda n, s, d: nc.dram_tensor(n, s, d, kind="ExternalInput").ap()
    do = lambda n, s, d: nc.dram_tensor(n, s, d, kind="ExternalOutput").ap()
    x1 = di("x1", [TPC, D], F32)
    xn1t = di("xn1t", [D, TPC], BF16)
    o1t = di("o1t", [D, TPC], BF16)
    wo1 = di("wo1", [D, D], F32)
    a1cat = di("a1cat", [D, 128], F32)
    b1cat = di("b1cat", [128, D], F32)
    wrt = di("wrt", [D, 8], F32)
    brr = di("brr", [1, 8], F32)
    e8 = di("e8", [8, 128], F32)
    identity = di("identity", [128, 128], F32)
    out = do("out", [TPC, D], F32)

    with ExitStack() as ctx:
        sb = lambda n, s, d: ctx.enter_context(nc.sbuf_tensor(n, s, d))
        psum = lambda n, s, d: ctx.enter_context(nc.psum_tensor(n, s, d))
        xn1t_sb = sb("xn1t_sb", [128, 32 * TPC], BF16)
        wblk = sb("wblk", [128, 32 * 512], BF16)
        g_blk = sb("g_blk", [128, D], F32)
        o1_tt = sb("o1_tt", [128, 32 * 128], BF16)
        a1_sb = sb("a1_sb", [128, 32 * 128], BF16)
        b1_sb = sb("b1_sb", [128, D], BF16)
        wr_sb = sb("wr_sb", [128, 32 * 8], BF16)
        e8_sb = sb("e8_sb", [8, 128], BF16)
        br_sb = sb("br_sb", [1, 8], F32)
        ones_sb = sb("ones_sb", [1, 128], F32)
        ident_sb = sb("ident_sb", [128, 128], BF16)
        p8_sb = sb("p8_sb", [128, 8], F32)
        negm8 = sb("negm8", [128, 1], F32)
        z8 = sb("z8", [128, 1], F32)
        rz8 = sb("rz8", [128, 1], F32)
        sc8 = sb("sc8", [128, 8], BF16)
        sct_sb = sb("sct_sb", [8, 128], BF16)
        scb_sb = sb("scb_sb", [128, 128], F32)
        lows_sb = sb("lows_sb", [128, TPC], BF16)
        xt0 = sb("xt0", [128, 512], F32)
        xt1 = sb("xt1", [128, 512], F32)
        tmp0 = sb("tmp0", [128, 512], F32)
        tmp1 = sb("tmp1", [128, 512], F32)
        ot0 = sb("ot0", [128, 512], F32)
        ot1 = sb("ot1", [128, 512], F32)
        ps_r1 = psum("ps_r1", [128, 512], F32)
        ps_r2 = psum("ps_r2", [128, 512], F32)
        ps_t8 = psum("ps_t8", [8, 128], BF16)
        ps_a = psum("ps_a", [128, 512], F32)
        ps_b = psum("ps_b", [128, 512], F32)
        dscr = psum("dscr", [128, 1024], BF16)

        P = Plan(nc)
        P.settle_fn = lambda e: e.transpose(dscr[:, 0:128], ident_sb[:],
                                            ident_sb[:])
        hid = P.dma("gpsimd", ident_sb[:], identity)
        hxn = P.dma("sync", xn1t_sb[:].rearrange("p (k t) -> p k t", k=32),
                    xn1t.rearrange("(k p) t -> p k t", p=128))
        ha1 = P.dma("gpsimd", a1_sb[:].rearrange("p (k c) -> p k c", k=32),
                    a1cat.rearrange("(k p) c -> p k c", p=128))
        hb1 = P.dma("gpsimd", b1_sb[:], b1cat)
        hwr = P.dma("gpsimd", wr_sb[:].rearrange("p (k c) -> p k c", k=32),
                    wrt.rearrange("(k p) c -> p k c", p=128))
        he8 = P.dma("gpsimd", e8_sb[:], e8)
        hbr = P.dma("sync", br_sb[:], brr)
        hones = P.op("vector", lambda e: e.memset(ones_sb[:], 1.0))

        war = {}
        lows_done = []
        for tt in range(8):
            def logmm(e, tt=tt):
                r = e.matmul(ps_r1[:, 0:8], ones_sb[:], br_sb[:],
                             start=True, stop=False)
                for kc in range(32):
                    r = e.matmul(ps_r1[:, 0:8],
                                 xn1t_sb[:, kc * TPC + tt * 128:
                                         kc * TPC + (tt + 1) * 128],
                                 wr_sb[:, kc * 8:(kc + 1) * 8],
                                 start=False, stop=(kc == 31))
                return r
            hlg = P.op("tensor", logmm,
                       [hxn, hwr, hbr, hones, hid, war.get("ps_r1")])
            hmx = P.op("vector", lambda e: e.reduce_max(
                negm8[:], ps_r1[:, 0:8], axis=AX.X, negate=True),
                [hlg, war.get("negm8")])
            hex_ = P.op("scalar", lambda e: e.activation(
                p8_sb[:], ps_r1[:, 0:8], AF.Exp, bias=negm8[:],
                accum_out=z8[:]), [hmx, war.get("p8")])
            war["negm8"] = hex_
            hrz = P.op("vector", lambda e: e.reciprocal(rz8[:], z8[:]),
                       [hex_, war.get("rz8")])
            hrz2 = P.op("vector", lambda e: e.tensor_scalar(
                rz8[:], rz8[:], 8.0, None, ALU.mult), [hrz])
            hsc = P.op("vector", lambda e: e.tensor_scalar(
                sc8[:], p8_sb[:], rz8[:], None, ALU.mult),
                [hrz2, war.get("sc8")])
            war["p8"] = hsc
            war["rz8"] = hsc
            htr = P.op("tensor", lambda e: e.transpose(
                ps_t8[:], sc8[:], ident_sb[:]),
                [hsc, hid, war.get("ps_t8")])
            war["sc8"] = htr
            hev = P.op("scalar", lambda e: e.copy(sct_sb[:], ps_t8[:]),
                       [htr, war.get("sct")])
            war["ps_t8"] = hev
            hsb = P.op("tensor", lambda e: e.matmul(
                ps_r1[:, 128:256], e8_sb[:], sct_sb[:],
                start=True, stop=True), [hev, he8, war.get("ps_r1b")])
            war["sct"] = hsb
            war["ps_r1"] = hsb
            hscb = P.op("vector", lambda e: e.tensor_copy(
                scb_sb[:], ps_r1[:, 128:256]), [hsb, war.get("scb")])
            war["ps_r1b"] = hscb

            def lowmm(e, tt=tt):
                r = None
                for kc in range(32):
                    r = e.matmul(ps_r2[:, 0:128],
                                 a1_sb[:, kc * 128:(kc + 1) * 128],
                                 xn1t_sb[:, kc * TPC + tt * 128:
                                         kc * TPC + (tt + 1) * 128],
                                 start=(kc == 0), stop=(kc == 31))
                return r
            hlo = P.op("tensor", lowmm, [hxn, ha1, war.get("ps_r2")])
            hls = P.op("vector", lambda e, tt=tt: e.tensor_tensor(
                lows_sb[:, tt * 128:(tt + 1) * 128], ps_r2[:, 0:128],
                scb_sb[:], ALU.mult), [hlo, hscb, war.get("lows")])
            war["ps_r2"] = hls
            war["scb"] = hls
            lows_done.append(hls)
        hlows = fences_list(lows_done)

        war_wblk = [None]
        war_o1 = [None]
        war_ps = [None, None]
        war_xt = [None, None]
        war_tmp = [None, None]
        war_ot = [None, None]
        war_g = [None]
        out_stores = []
        for ob in range(8):
            hwo_ = P.dma("gpsimd", wblk[:].rearrange("p (k c) -> p k c", k=32),
                         wo1.rearrange("(k p) o -> p k o", p=128)
                         [:, :, ob * 512:(ob + 1) * 512], [war_wblk[0]])
            g_done = []
            for tt in range(8):
                pi = tt % 2
                pst = ps_a if pi == 0 else ps_b
                hg = P.op("tensor", lambda e, tt=tt, ob=ob, pst=pst: e.matmul(
                    pst[:], lows_sb[:, tt * 128:(tt + 1) * 128],
                    b1_sb[:, ob * 512:(ob + 1) * 512],
                    start=True, stop=True),
                    hlows + [hb1, war_ps[pi]])
                hsil = P.op("scalar", lambda e, tt=tt, pst=pst: e.activation(
                    g_blk[:, tt * 512:(tt + 1) * 512], pst[:], AF.Silu,
                    scale=2.0), [hg, war_g[0]])
                war_ps[pi] = hsil
                g_done.append(hsil)
            g_all = fences_list(g_done)
            for tt in range(8):
                hol = P.dma("sync",
                            o1_tt[:].rearrange("p (k c) -> p k c", k=32),
                            o1t.rearrange("(k p) t -> p k t", p=128)
                            [:, :, tt * 128:(tt + 1) * 128], [war_o1[0]])
                pi = tt % 2
                pst = ps_a if pi == 0 else ps_b

                def womm(e, tt=tt, ob=ob, pst=pst):
                    r = None
                    for kc in range(32):
                        r = e.matmul(pst[:],
                                     o1_tt[:, kc * 128:(kc + 1) * 128],
                                     wblk[:, kc * 512:(kc + 1) * 512],
                                     start=(kc == 0), stop=(kc == 31))
                    return r
                hmm = P.op("tensor", womm, [hol, hwo_, war_ps[pi]])
                war_o1[0] = hmm
                bi = tt % 2
                hxt = P.dma("sync", (xt0 if bi == 0 else xt1)[:],
                            x1[tt * 128:(tt + 1) * 128,
                               ob * 512:(ob + 1) * 512], [war_xt[bi]])
                htm = P.op("vector", lambda e, tt=tt, pst=pst, bi=bi:
                           e.tensor_tensor((tmp0 if bi == 0 else tmp1)[:],
                                           pst[:],
                                           g_blk[:, tt * 512:(tt + 1) * 512],
                                           ALU.mult),
                           [hmm, war_tmp[bi]] + g_all)
                war_ps[pi] = htm
                hx1 = P.op("vector", lambda e, bi=bi: e.tensor_tensor(
                    (ot0 if bi == 0 else ot1)[:],
                    (tmp0 if bi == 0 else tmp1)[:],
                    (xt0 if bi == 0 else xt1)[:], ALU.add),
                    [htm, hxt, war_ot[bi]])
                war_tmp[bi] = hx1
                war_xt[bi] = hx1
                hst = P.dma("sync", out[tt * 128:(tt + 1) * 128,
                                        ob * 512:(ob + 1) * 512],
                            (ot0 if bi == 0 else ot1)[:], [hx1])
                war_ot[bi] = hst
                out_stores.append(hst)
            war_wblk[0] = hmm
            war_g[0] = fences_list([war_ps[0], war_ps[1]])[0]
        P.barrier("sync", fences_list(out_stores))
        block = ctx.enter_context(nc.Block())
        P.emit(ctx, block)
    return nc

